# version 01 (final)
# baseline (speedup 1.0000x reference)
import numpy as np
import ml_dtypes
from contextlib import ExitStack
import concourse.bass as bass
import concourse.mybir as mybir
from concourse.bass import IndirectOffsetOnAxis
from concourse.bass_utils import run_bass_kernel_spmd

F32 = mybir.dt.float32
BF16 = mybir.dt.bfloat16
I32 = mybir.dt.int32
U32 = mybir.dt.uint32
U8 = mybir.dt.uint8
AF = mybir.ActivationFunctionType
ALU = mybir.AluOpType
AX = mybir.AxisListType
BFNP = ml_dtypes.bfloat16

T, D, H, HD, NT, NC = 2048, 1024, 16, 64, 16, 8
E, EL, CAP, NE_T = 32, 4, 640, 20480
C_SHIFT = 20.0
NEG = -30000.0
RG = [list(range(8))]

TRUNC = False
NHEADS = 16
bfc = lambda x: np.ascontiguousarray(np.asarray(x, np.float32)).astype(BFNP)


def make_consts():
    c = {}
    c["ident"] = bfc(np.eye(128))
    c["identf"] = np.eye(128, dtype=np.float32)
    c["tri"] = bfc(np.triu(np.ones((128, 128))))
    c["onesb"] = bfc(np.ones((128, 128)))
    c["onesf"] = np.ones((1, 64), np.float32)
    ki = np.arange(128)[:, None]
    jj = np.arange(128)[None, :]
    c["mtri"] = bfc(np.where(jj >= ki, 0.0, NEG))
    c["manti"] = bfc(np.where(ki > jj, 0.0, NEG))
    c["iota32"] = np.broadcast_to(np.arange(32, dtype=np.float32), (128, 32)).copy()
    half = HD // 2
    theta = 1.0 / 10000.0 ** (np.arange(half, dtype=np.float32) / half)
    fr = np.arange(T, dtype=np.float32)[:, None] * theta[None, :]
    cosf = np.cos(np.concatenate([fr, fr], 1)).astype(np.float32)
    sinf = np.sin(fr).astype(np.float32)
    c["cosb"] = cosf.reshape(NT, 128, HD).transpose(1, 0, 2).reshape(128, NT * HD).copy()
    c["sinn"] = (-sinf).reshape(NT, 128, half).transpose(1, 0, 2).reshape(128, NT * half).copy()
    c["sinp"] = sinf.reshape(NT, 128, half).transpose(1, 0, 2).reshape(128, NT * half).copy()
    return c


class Cnt:
    def __init__(self, sem):
        self.sem, self.v = sem, 0

    def inc(self, instr, n=1):
        instr.then_inc(self.sem, n)
        self.v += n
        return self.v


def build(dbg=False):
    nc = bass.Bass("TRN2", target_bir_lowering=False)
    CNP = make_consts()

    x_in = nc.dram_tensor("x", [T, D], F32, kind="ExternalInput").ap()
    wqkv_in = nc.dram_tensor("wqkv", [D, 3 * D], F32, kind="ExternalInput").ap()
    wout_in = nc.dram_tensor("wout", [D, D], F32, kind="ExternalInput").ap()
    wr_in = nc.dram_tensor("wr", [D, E], F32, kind="ExternalInput").ap()
    wg_in = nc.dram_tensor("wg", [EL, D, D], F32, kind="ExternalInput").ap()
    wu_in = nc.dram_tensor("wu", [EL, D, D], F32, kind="ExternalInput").ap()
    wd_in = nc.dram_tensor("wd", [EL, D, D], F32, kind="ExternalInput").ap()
    cid_in = nc.dram_tensor("cid", [1, 1], I32, kind="ExternalInput").ap()
    out_d = nc.dram_tensor("out", [T, D], F32, kind="ExternalOutput").ap()

    CD = {k: nc.inline_tensor(v, "c_" + k).ap() for k, v in CNP.items()}

    xn_d = nc.dram_tensor("xn_d", [T, D], F32).ap()
    xnbf_d = nc.dram_tensor("xnbf_d", [T, D], BF16).ap()
    ids_own_d = nc.dram_tensor("ids_own_d", [T, 1], F32).ap()
    ids_all_d = nc.dram_tensor("ids_all_d", [8 * T, 1], F32, addr_space="Shared").ap()
    xs_own_d = nc.dram_tensor("xs_own_d", [T, D], BF16).ap()
    xs_all_d = nc.dram_tensor("xs_all_d", [8 * T, D], BF16, addr_space="Shared").ap()
    y_own_d = nc.dram_tensor("y_own_d", [EL * CAP, D], BF16).ap()
    y_all_d = nc.dram_tensor("y_all_d", [NE_T, D], BF16, addr_space="Shared").ap()
    counts_d = nc.dram_tensor("counts_d", [128, 32], F32).ap()
    gcar_d = nc.dram_tensor("gcar_d", [128, 32], F32).ap()
    cmp_d = nc.dram_tensor("cmp_d", [128, 32], F32).ap()
    gs8_d = nc.dram_tensor("gs8_d", [E, 8], F32).ap()
    dl8_d = nc.dram_tensor("dl8_d", [E, 8], F32).ap()

    DOUT = {}
    if dbg:
        for nm, shape, dt in [
            ("d_xn", [T, D], F32), ("d_ids", [T, 1], F32),
            ("d_qkb", [128, 2 * 2048], F32), ("d_vext", [128, 2 * 1040], F32),
            ("d_aoT", [128, 8 * 2048], F32), ("d_counts", [128, 32], F32),
            ("d_gcar", [128, 32], F32), ("d_cmp", [128, 32], F32),
            ("d_gs8", [E, 8], F32), ("d_dl8", [E, 8], F32),
            ("d_gslot", [128, 16], F32), ("d_bufidx", [128, 20], F32),
            ("d_gates", [128, 16], F32), ("d_xs", [T, D], F32),
            ("d_probs", [128, 6 * 640], F32), ("d_tT", [128, 4 * 2048], F32),
            ("d_rcpB", [64, 512], F32),
            ("d_y", [EL * CAP, D], F32),
        ]:
            DOUT[nm] = nc.dram_tensor(nm, shape, dt, kind="ExternalOutput").ap()

    pstack = ExitStack()
    SBP = lambda name, shape, dt: pstack.enter_context(nc.sbuf_tensor(name, shape, dt))

    with pstack, nc.Block() as blk:
        _V0, _SC0, PE, _GP0, SY = (blk.vector, blk.scalar, blk.tensor,
                                   blk.gpsimd, blk.sync)

        def V(f):
            def g(v):
                v.wait_ge(sV.sem, sV.v)
                f(v)
            _V0(g)

        def SC(f):
            def g(s):
                s.wait_ge(sS.sem, sS.v)
                f(s)
            _SC0(g)

        def GP(f):
            def g(x):
                x.wait_ge(sL.sem, sL.v)
                f(x)
            _GP0(g)

        ident = SBP("ident", [128, 128], BF16)
        identf = SBP("identf", [128, 128], F32)
        tri = SBP("tri", [128, 128], BF16)
        onesb = SBP("onesb", [128, 128], BF16)
        onesf = SBP("onesf", [1, 64], F32)
        mtri = SBP("mtri", [128, 128], BF16)
        manti = SBP("manti", [128, 128], BF16)
        iota32 = SBP("iota32", [128, 32], F32)
        cosb = SBP("cosb", [128, NT * HD], F32)
        sinn = SBP("sinn", [128, NT * 32], F32)
        sinp = SBP("sinp", [128, NT * 32], F32)
        cshift = SBP("cshift", [128, 1], F32)
        epst = SBP("epst", [128, 1], F32)
        zt = SBP("zt", [128, 1], F32)
        c2e4 = SBP("c2e4", [128, 1], F32)
        gates = SBP("gates", [128, 16], F32)
        idsT_own = SBP("idsT_own", [128, 16], F32)
        gslotI = SBP("gslotI", [128, 16], I32)
        cslotI = SBP("cslotI", [128, 16], I32)
        bufidxI = SBP("bufidxI", [128, 20], I32)
        gcarB = SBP("gcarB", [128, 512], F32)
        cmpB = SBP("cmpB", [128, 512], F32)
        gsB = SBP("gsB", [128, 32], F32)
        dlB = SBP("dlB", [128, 32], F32)
        wr_bf = SBP("wr_bf", [128, NC * 32], BF16)
        xnb2 = SBP("xnb2", [128, 2 * 1024], BF16)
        pat = ExitStack()
        v_ext = pat.enter_context(nc.sbuf_tensor("v_ext", [128, NT * (H * 65)], BF16))
        qkb = pat.enter_context(nc.sbuf_tensor("qkb", [128, NT * 2048], BF16))

        scoreP = [pstack.enter_context(nc.psum_tensor(f"scp{i}", [128, 640], F32))
                  for i in range(2)]
        psE = pstack.enter_context(nc.psum_tensor("psE", [128, 512], F32))
        psF = pstack.enter_context(nc.psum_tensor("psF", [128, 512], F32))
        psT = pstack.enter_context(nc.psum_tensor("psT", [128, 1024], BF16))
        psG = pstack.enter_context(nc.psum_tensor("psG", [128, 512], F32))

        sCst = Cnt(pstack.enter_context(nc.semaphore("sCst")))
        sX = [Cnt(pstack.enter_context(nc.semaphore(f"sX{i}"))) for i in range(2)]
        sXn = [Cnt(pstack.enter_context(nc.semaphore(f"sXn{i}"))) for i in range(2)]
        sIds = Cnt(pstack.enter_context(nc.semaphore("sIds")))
        sWqk = Cnt(pstack.enter_context(nc.semaphore("sWqk")))
        sWv = Cnt(pstack.enter_context(nc.semaphore("sWv")))
        sWr = Cnt(pstack.enter_context(nc.semaphore("sWr")))
        sWgu = Cnt(pstack.enter_context(nc.semaphore("sWgu")))
        sWd = Cnt(pstack.enter_context(nc.semaphore("sWd")))
        sBuf = Cnt(pstack.enter_context(nc.semaphore("sBuf")))
        sCnt2 = Cnt(pstack.enter_context(nc.semaphore("sCnt2")))
        sTbl = Cnt(pstack.enter_context(nc.semaphore("sTbl")))
        sBc = Cnt(pstack.enter_context(nc.semaphore("sBc")))
        sIdsL = Cnt(pstack.enter_context(nc.semaphore("sIdsL")))
        sSc = Cnt(pstack.enter_context(nc.semaphore("sSc")))
        sF = [Cnt(pstack.enter_context(nc.semaphore(f"sF{i}"))) for i in range(2)]
        sGb = [Cnt(pstack.enter_context(nc.semaphore(f"sGb{i}"))) for i in range(2)]
        sY = Cnt(pstack.enter_context(nc.semaphore("sY")))
        sOs = [Cnt(pstack.enter_context(nc.semaphore(f"sOs{i}"))) for i in range(2)]
        sWo = Cnt(pstack.enter_context(nc.semaphore("sWo")))
        sP = Cnt(pstack.enter_context(nc.semaphore("sP")))
        sV = Cnt(pstack.enter_context(nc.semaphore("sV")))
        sS = Cnt(pstack.enter_context(nc.semaphore("sS")))
        sO = Cnt(pstack.enter_context(nc.semaphore("sO")))
        sL = Cnt(pstack.enter_context(nc.semaphore("sL")))
        sDbg = Cnt(pstack.enter_context(nc.semaphore("sDbg")))
        cc1 = pstack.enter_context(nc.semaphore("cc1"))
        cc2 = pstack.enter_context(nc.semaphore("cc2"))
        cc3 = pstack.enter_context(nc.semaphore("cc3"))

        M = {}
        last_psT = [None]

        def transpose_batch(srcs, dst_ap, wait_v=None, wait_pe=None):
            n = len(srcs)
            assert n <= 8
            lp = last_psT[0]

            def pe(t):
                if lp is not None:
                    t.wait_ge(sV.sem, lp)
                if wait_pe:
                    for s_, v_ in wait_pe:
                        t.wait_ge(s_, v_)
                last = None
                for i, a in enumerate(srcs):
                    last = t.matmul(psT[:, i * 128:(i + 1) * 128], a, ident[:],
                                    is_transpose=True, start=(i == 0),
                                    stop=(i == n - 1))
                sP.inc(last)

            PE(pe)
            mpe = sP.v

            def ve(v):
                v.wait_ge(sP.sem, mpe)
                if wait_v:
                    for s_, v_ in wait_v:
                        v.wait_ge(s_, v_)
                sV.inc(v.tensor_copy(dst_ap, psT[:, 0:128 * n]))

            V(ve)
            last_psT[0] = sV.v
            return mpe, sV.v

        p0 = ExitStack()
        SB0 = lambda name, shape, dt: p0.enter_context(nc.sbuf_tensor(name, shape, dt))
        x4 = SB0("x4", [128, 2 * 1024], F32)
        xnf2 = SB0("xnf2", [128, 2 * 1024], F32)
        stats = SB0("stats", [128, 12], F32)
        mv = SB0("mv", [128, 2], F32)
        lnt = SB0("lnt", [128, 1], F32)
        rstd = SB0("rstd", [128, 1], F32)
        xnTt = SB0("xnTt", [128, 2 * 1024], BF16)
        wqk_bf = SB0("wqk_bf", [128, NC * 2048], BF16)
        wv_bf = SB0("wv_bf", [128, NC * 1024], BF16)
        qkf = SB0("qkf", [128, 2 * 1024], F32)
        rt1 = SB0("rt1", [128, 1024], F32)
        rt2 = SB0("rt2", [128, 1024], F32)
        logit_s = SB0("logit_s", [128, 32], F32)
        lsub = SB0("lsub", [128, 32], F32)
        lexp = SB0("lexp", [128, 32], F32)
        max8 = SB0("max8", [128, 8], F32)
        idx8 = SB0("idx8", [128, 8], U32)
        sumx = SB0("sumx", [128, 1], F32)

        wqkv_r = wqkv_in.rearrange("(a p) d -> p a d", p=128)
        wr_r = wr_in.rearrange("(a p) e -> p a e", p=128)

        @SY
        def _(s):
            for nm, tile in [("ident", ident), ("identf", identf), ("tri", tri), ("onesb", onesb),
                             ("onesf", onesf), ("mtri", mtri), ("manti", manti),
                             ("iota32", iota32), ("cosb", cosb), ("sinn", sinn),
                             ("sinp", sinp)]:
                sCst.inc(s.dma_start(out=tile[:], in_=CD[nm]), 16)
            M["consts"] = sCst.v

        @GP
        def _(g):
            sWr.inc(g.dma_start(out=wr_bf[:], in_=wr_r), 16)
            M["wrbf"] = sWr.v
            for k in range(NC):
                sWqk.inc(g.dma_start(out=wqk_bf[:, k * 2048:(k + 1) * 2048],
                                     in_=wqkv_r[:, k, 0:2048]), 16)
            M["wqkbf"] = sWqk.v
            for k in range(NC):
                sWv.inc(g.dma_start(out=wv_bf[:, k * 1024:(k + 1) * 1024],
                                    in_=wqkv_r[:, k, 2048:3072]), 16)
            M["wvbf"] = sWv.v

        @V
        def _(v):
            v.memset(cshift[:], -C_SHIFT)
            v.memset(epst[:], 1e-5)
            v.memset(zt[:], 0.0)
            v.memset(c2e4[:], float(NE_T))
            va = v_ext[:].ap[0][0]
            sV.inc(v.memset(bass.AP(v_ext, 64, [[va, 128], [65, NT * H], [1, 1]]),
                            1.0))
            M["vinit"] = sV.v

        for i in range(NT):
            sl = i % 2

            def sy_x(s, i=i, sl=sl):
                if i >= 2:
                    s.wait_ge(sV.sem, M[f"xn{i - 2}"])
                sX[sl].inc(s.dma_start(out=x4[:, sl * 1024:(sl + 1) * 1024],
                                        in_=x_in[i * 128:(i + 1) * 128, :]), 16)
                M[f"x{i}"] = sX[sl].v

            SY(sy_x)

            def v_stats(v, i=i, sl=sl):
                v.wait_ge(sX[sl].sem, M[f"x{i}"])
                if i > 0:
                    v.wait_ge(sS.sem, M[f"rstd{i - 1}"])
                v.bn_stats(stats[:, 0:6], x4[:, sl * 1024:sl * 1024 + 512])
                sV.inc(v.bn_stats(stats[:, 6:12],
                                  x4[:, sl * 1024 + 512:(sl + 1) * 1024]))
                v.wait_ge(sV.sem, sV.v)
                sV.inc(v.bn_aggr(mv[:], stats[:]))
                M[f"bna{i}"] = sV.v

            V(v_stats)

            def sc_rstd(s, i=i):
                s.wait_ge(sV.sem, M[f"bna{i}"])
                sS.inc(s.activation(lnt[:], mv[:, 1:2], AF.Ln, bias=epst[:],
                                    scale=1.0))
                s.wait_ge(sS.sem, sS.v)
                sS.inc(s.activation(rstd[:], lnt[:], AF.Exp, bias=zt[:], scale=-0.5))
                M[f"rstd{i}"] = sS.v

            SC(sc_rstd)

            def v_norm(v, i=i, sl=sl):
                v.wait_ge(sS.sem, M[f"rstd{i}"])
                if i >= 2:
                    v.wait_ge(sXn[sl].sem, M[f"xnst{i - 2}"])
                    v.wait_ge(sP.sem, M[f"xnTmm{i - 2}"])
                sV.inc(v.tensor_scalar(xnf2[:, sl * 1024:(sl + 1) * 1024],
                                       x4[:, sl * 1024:(sl + 1) * 1024],
                                       mv[:, 0:1], rstd[:], ALU.subtract, ALU.mult))
                M[f"xn{i}"] = sV.v
                v.wait_ge(sV.sem, sV.v)
                sV.inc(v.tensor_copy(xnb2[:, sl * 1024:(sl + 1) * 1024],
                                     xnf2[:, sl * 1024:(sl + 1) * 1024]))
                M[f"xnb{i}"] = sV.v

            V(v_norm)

            def sc_store(s, i=i, sl=sl):
                s.wait_ge(sV.sem, M[f"xnb{i}"])
                sXn[sl].inc(s.dma_start(out=xn_d[i * 128:(i + 1) * 128, :],
                                        in_=xnf2[:, sl * 1024:(sl + 1) * 1024]), 16)
                sXn[sl].inc(s.dma_start(out=xnbf_d[i * 128:(i + 1) * 128, :],
                                        in_=xnb2[:, sl * 1024:(sl + 1) * 1024]), 16)
                M[f"xnst{i}"] = sXn[sl].v

            SC(sc_store)

            srcs = [xnb2[:, sl * 1024 + kk * 128: sl * 1024 + (kk + 1) * 128]
                    for kk in range(8)]
            wv_list = [(sCst.sem, M["consts"])]
            if i >= 2:
                wv_list += [(sP.sem, M[f"qk_mm{i - 2}"]), (sP.sem, M[f"v_mm{i - 2}"]),
                            (sP.sem, M[f"rt_mm{i - 2}"])]

            def pe_wait_xnb(t, i=i):
                t.wait_ge(sCst.sem, M["consts"])
                t.wait_ge(sV.sem, M[f"xnb{i}"])

            PE(pe_wait_xnb)
            mpe, mev = transpose_batch(srcs, xnTt[:, sl * 1024:(sl + 1) * 1024],
                                       wait_v=wv_list)
            M[f"xnTmm{i}"] = mpe
            M[f"xnT{i}"] = mev

            def pe_rt(t, i=i, sl=sl):
                t.wait_ge(sV.sem, M[f"xnT{i}"])
                t.wait_ge(sWr.sem, M["wrbf"])
                if i > 0:
                    t.wait_ge(sV.sem, M[f"lgev{i - 1}"])
                last = None
                for k in range(NC):
                    last = t.matmul(
                        psG[:, 0:32],
                        xnTt[:, sl * 1024 + k * 128: sl * 1024 + (k + 1) * 128],
                        wr_bf[:, k * 32:(k + 1) * 32],
                        start=(k == 0), stop=(k == NC - 1))
                sP.inc(last)
                M[f"rt_mm{i}"] = sP.v

            PE(pe_rt)

            def v_logit(v, i=i):
                v.wait_ge(sP.sem, M[f"rt_mm{i}"])
                sV.inc(v.tensor_copy(logit_s[:], psG[:, 0:32]))
                M[f"lgev{i}"] = sV.v
                v.wait_ge(sV.sem, sV.v)
                sV.inc(v.max(max8[:], logit_s[:]))
                v.wait_ge(sV.sem, sV.v)
                sV.inc(v.max_index(idx8[:], max8[:], logit_s[:]))
                v.wait_ge(sV.sem, sV.v)
                sV.inc(v.tensor_scalar(idsT_own[:, i:i + 1], idx8[:, 0:1], 0.0,
                                       None, ALU.add))
                v.wait_ge(sV.sem, sV.v)
                sV.inc(v.tensor_scalar(lsub[:], logit_s[:], max8[:, 0:1], None,
                                       ALU.subtract))
                M[f"lsub{i}"] = sV.v

            V(v_logit)

            def sc_gate(s, i=i):
                s.wait_ge(sV.sem, M[f"lsub{i}"])
                sS.inc(s.activation(lexp[:], lsub[:], AF.Exp, bias=zt[:], scale=1.0,
                                    accum_out=sumx[:]))
                M[f"sume{i}"] = sS.v
                sIds.inc(s.dma_start(out=ids_own_d[i * 128:(i + 1) * 128, :],
                                     in_=idsT_own[:, i:i + 1]), 16)
                M[f"idst{i}"] = sIds.v

            SC(sc_gate)

            def v_gate(v, i=i):
                v.wait_ge(sS.sem, M[f"sume{i}"])
                sV.inc(v.reciprocal(gates[:, i:i + 1], sumx[:]))
                M[f"gate{i}"] = sV.v

            V(v_gate)

            def pe_v(t, i=i, sl=sl):
                t.wait_ge(sWv.sem, M["wvbf"])
                t.wait_ge(sV.sem, M[f"xnT{i}"])
                if i > 0:
                    t.wait_ge(sV.sem, M[f"vext{i - 1}"])
                last = None
                for k in range(NC):
                    lhs = xnTt[:, sl * 1024 + k * 128: sl * 1024 + (k + 1) * 128]
                    t.matmul(psE[:, 0:512], lhs, wv_bf[:, k * 1024:k * 1024 + 512],
                             start=(k == 0), stop=(k == NC - 1))
                    last = t.matmul(psF[:, 0:512], lhs,
                                    wv_bf[:, k * 1024 + 512:(k + 1) * 1024],
                                    start=(k == 0), stop=(k == NC - 1))
                sP.inc(last)
                M[f"v_mm{i}"] = sP.v

            PE(pe_v)

            def v_vext(v, i=i):
                v.wait_ge(sP.sem, M[f"v_mm{i}"])
                v.wait_ge(sV.sem, M["vinit"])
                va = v_ext[:].ap[0][0]
                v.tensor_copy(
                    bass.AP(v_ext, i * (H * 65), [[va, 128], [65, 8], [1, 64]]),
                    psE[:, 0:512])
                sV.inc(v.tensor_copy(
                    bass.AP(v_ext, i * (H * 65) + 8 * 65,
                            [[va, 128], [65, 8], [1, 64]]),
                    psF[:, 0:512]))
                M[f"vext{i}"] = sV.v

            V(v_vext)

            for hf in range(2):
                def pe_qk(t, i=i, sl=sl, hf=hf):
                    t.wait_ge(sWqk.sem, M["wqkbf"])
                    t.wait_ge(sV.sem, M[f"xnT{i}"])
                    prev = M.get(f"qkevac{i}_{hf - 1}") or M.get(f"qkevac{i - 1}_1")
                    if prev:
                        t.wait_ge(sS.sem, prev)
                    last = None
                    for k in range(NC):
                        lhs = xnTt[:, sl * 1024 + k * 128: sl * 1024 + (k + 1) * 128]
                        base = k * 2048 + hf * 1024
                        t.matmul(scoreP[0][:, 0:512], lhs,
                                 wqk_bf[:, base:base + 512],
                                 start=(k == 0), stop=(k == NC - 1))
                        last = t.matmul(scoreP[1][:, 0:512], lhs,
                                        wqk_bf[:, base + 512:base + 1024],
                                        start=(k == 0), stop=(k == NC - 1))
                    sP.inc(last)
                    M[f"qk_mm{i}_{hf}"] = sP.v
                    if hf == 1:
                        M[f"qk_mm{i}"] = sP.v

                PE(pe_qk)

                def sc_qkev(s, i=i, hf=hf):
                    s.wait_ge(sP.sem, M[f"qk_mm{i}_{hf}"])
                    prevrope = M.get(f"rope{i}_{hf - 1}") or M.get(f"rope{i - 1}_1")
                    if prevrope:
                        s.wait_ge(sV.sem, prevrope)
                    s.copy(qkf[:, hf * 1024:hf * 1024 + 512], scoreP[0][:, 0:512])
                    sS.inc(s.copy(qkf[:, hf * 1024 + 512:(hf + 1) * 1024],
                                  scoreP[1][:, 0:512]))
                    M[f"qkevac{i}_{hf}"] = sS.v

                SC(sc_qkev)

                def v_rope(v, i=i, hf=hf):
                    v.wait_ge(sS.sem, M[f"qkevac{i}_{hf}"])
                    pr = M.get(f"rope{i}_{hf - 1}") or M.get(f"rope{i - 1}_1")
                    if pr:
                        v.wait_ge(sV.sem, pr)
                    qh = qkf[:, hf * 1024:(hf + 1) * 1024]
                    _q = qkf[:].ap[0][0]
                    _c = cosb[:].ap[0][0]
                    _s = sinn[:].ap[0][0]
                    _t2 = rt2[:].ap[0][0]
                    v.tensor_tensor(
                        rt1[:], qh,
                        bass.AP(cosb, i * HD, [[_c, 128], [0, 16], [1, 64]]),
                        ALU.mult)
                    v.tensor_tensor(
                        bass.AP(rt2, 0, [[_t2, 128], [64, 16], [1, 32]]),
                        bass.AP(qkf, hf * 1024 + 32, [[_q, 128], [64, 16], [1, 32]]),
                        bass.AP(sinn, i * 32, [[_s, 128], [0, 16], [1, 32]]),
                        ALU.mult)
                    v.tensor_tensor(
                        bass.AP(rt2, 32, [[_t2, 128], [64, 16], [1, 32]]),
                        bass.AP(qkf, hf * 1024, [[_q, 128], [64, 16], [1, 32]]),
                        bass.AP(sinp, i * 32, [[_s, 128], [0, 16], [1, 32]]),
                        ALU.mult).then_inc(sV.sem, 1)
                    sV.v += 1
                    v.wait_ge(sV.sem, sV.v)
                    sV.inc(v.tensor_add(
                        qkb[:, i * 2048 + hf * 1024: i * 2048 + (hf + 1) * 1024],
                        rt1[:], rt2[:]))
                    M[f"rope{i}_{hf}"] = sV.v
                    if hf == 1:
                        M[f"rope{i}"] = sV.v

                V(v_rope)

        @GP
        def _(g):
            g.wait_ge(sIds.sem, M[f"idst{NT - 1}"])
            g.collective_compute("AllGather", ALU.bypass, replica_groups=RG,
                                 ins=[ids_own_d], outs=[ids_all_d]).then_inc(cc1)

        M["p0_xn0"], M["p0_xn1"] = sXn[0].v, sXn[1].v
        M["p0_v"], M["p0_s"], M["p0_p"] = sV.v, sS.v, sP.v
        p0.close()

        def _attn_barrier(e):
            e.wait_ge(sXn[0].sem, M["p0_xn0"])
            e.wait_ge(sXn[1].sem, M["p0_xn1"])
            e.wait_ge(sP.sem, M["p0_p"])
            e.wait_ge(sS.sem, M["p0_s"])
            e.wait_ge(sV.sem, M["p0_v"])

        V(_attn_barrier)
        SC(_attn_barrier)
        GP(_attn_barrier)

        pao = ExitStack()
        aoT = pao.enter_context(nc.sbuf_tensor("aoT", [128, 8 * 2048], BF16,
                                               side="right"))

        pa = ExitStack()
        SBA = lambda name, shape, dt: pa.enter_context(nc.sbuf_tensor(name, shape, dt))
        tT = SBA("tT", [128, 4 * 2048], BF16)
        probs = SBA("probs", [128, 6 * 640], BF16)
        rcp_s = SBA("rcp_s", [1, 512], F32)
        rcpB = SBA("rcpB", [64, 512], F32)
        idsL = SBA("idsL", [128, 128], F32)
        idsTa = SBA("idsTa", [128, 128], F32)
        obA = SBA("obA", [128, 512], BF16)
        cntflat = SBA("cntflat", [1, 4096], F32)
        cnt128 = SBA("cnt128", [128, 32], F32)
        gcrow = SBA("gcrow", [128, 32], F32)
        cmprow = SBA("cmprow", [128, 32], F32)
        prefA = SBA("prefA", [128, 512], F32)
        oh1 = SBA("oh1", [128, 32], F32)
        sc1 = SBA("sc1", [128, 32], F32)
        sv1 = SBA("sv1", [128, 1], F32)
        sv2 = SBA("sv2", [128, 1], F32)
        sv3 = SBA("sv3", [128, 1], F32)
        svm = SBA("svm", [128, 1], U8)
        iota5 = SBA("iota5", [128, 5], F32)
        xsc = SBA("xsc", [128, 1024], BF16)
        cntT = SBA("cntT", [32, 128], F32)
        gincl = SBA("gincl", [32, 128], F32)
        gexcl = SBA("gexcl", [32, 128], F32)
        zrow = SBA("zrow", [32, 128], F32)
        rkcnt = SBA("rkcnt", [32, 8], F32)
        rk8 = SBA("rk8", [8, 32], F32)
        z8 = SBA("z8", [8, 32], F32)
        ls8i = SBA("ls8i", [8, 32], F32)
        ls8 = SBA("ls8", [8, 32], F32)
        ls32 = SBA("ls32", [32, 8], F32)
        gs32 = SBA("gs32", [32, 8], F32)
        dl32 = SBA("dl32", [32, 8], F32)
        dlD = SBA("dlD", [32, 8], F32)
        cmpT32 = SBA("cmpT32", [32, 128], F32)
        io8 = SBA("io8", [32, 8], F32)

        def emit_pair_transposes(p):
            buf = (p % 2) * 2 * 2048
            for half, colbase in ((0, 0), (1, 1024)):
                for bat in range(2):
                    srcs = []
                    for ii in range(8):
                        tile = bat * 8 + ii
                        srcs.append(qkb[:, tile * 2048 + colbase + p * 128:
                                        tile * 2048 + colbase + (p + 1) * 128])
                    wpe = [(sV.sem, M[f"rope{NT - 1}"])]
                    wl = []
                    if p >= 2:
                        wl.append((sP.sem, M[f"head_done{2 * (p - 2) + 1}"]))
                    dst = tT[:, buf + half * 2048 + bat * 1024:
                             buf + half * 2048 + (bat + 1) * 1024]
                    _, mev = transpose_batch(srcs, dst, wait_v=wl, wait_pe=wpe)
            M[f"tT{p}"] = sV.v

        def emit_head(h):
            p, sub = h // 2, h % 2
            buf = (p % 2) * 2 * 2048
            rowlo = sub * 64

            def qT_ap(c0, w):
                return tT[rowlo:rowlo + 64, buf + c0: buf + c0 + w]

            def kT_ap(c0, w):
                return tT[rowlo:rowlo + 64, buf + 2048 + c0: buf + 2048 + c0 + w]

            for step in range(NT + 1):
                if step < NT:
                    kap = step
                    w = min(640, 2048 - 128 * kap)
                    sp = scoreP[kap % 2]

                    def pe_s(t, h=h, kap=kap, w=w, sp=sp, p=p):
                        if kap == 0:
                            t.wait_ge(sV.sem, M[f"tT{p}"])
                        if h == 0 and kap < 2:
                            t.wait_ge(sS.sem, M[f"qkevac{NT - 1}_1"])
                        prev = M.get(f"exp{h}_{kap - 2}")
                        if prev is None and kap < 2 and h > 0:
                            prev = M[f"exp{h - 1}_{NT - 2 + kap}"]
                        if prev:
                            t.wait_ge(sS.sem, prev)
                        lhs = kT_ap(kap * 128, 128)
                        w1 = min(512, w)
                        t.matmul(sp[:, 0:w1], lhs, qT_ap(kap * 128, w1),
                                 start=True, stop=False)
                        if w > 512:
                            t.matmul(sp[:, 512:w], lhs,
                                     qT_ap(kap * 128 + 512, w - 512),
                                     start=True, stop=False)
                        last = t.matmul(sp[:, 0:128], ident[:], mtri[:],
                                        start=False, stop=True)
                        if w > 512:
                            last = t.matmul(sp[:, 512:w], ident[:],
                                            manti[:, 0:w - 512],
                                            start=False, stop=True)
                        sP.inc(last)
                        M[f"smm{h}_{kap}"] = sP.v

                    PE(pe_s)

                    def sc_exp(s, h=h, kap=kap, w=w, sp=sp):
                        s.wait_ge(sP.sem, M[f"smm{h}_{kap}"])
                        pv = M.get(f"av{h}_{kap - 2}")
                        if pv is None and kap < 2 and h > 0:
                            pv = M[f"av{h - 1}_{NT - 2 + kap}"]
                        if pv:
                            s.wait_ge(sP.sem, pv)
                        slot = kap % 6
                        sS.inc(s.activation(probs[:, slot * 640: slot * 640 + w],
                                            sp[0:128, 0:w], AF.Exp, bias=cshift[:],
                                            scale=0.125))
                        M[f"exp{h}_{kap}"] = sS.v

                    SC(sc_exp)

                if step >= 1:
                    phi = step - 1
                    blkid = phi // 4
                    avp = psE if blkid % 2 == 0 else psF
                    kaps = list(range(max(0, phi - 4), phi + 1))

                    def pe_a(t, h=h, phi=phi, avp=avp, kaps=kaps, blkid=blkid):
                        t.wait_ge(sS.sem, M[f"exp{h}_{phi}"])
                        if phi % 4 == 0:
                            if blkid >= 2:
                                t.wait_ge(sV.sem, M[f"nrm{h}_{phi - 5}"])
                            elif h > 0:
                                t.wait_ge(sV.sem,
                                          M[f"nrm{h - 1}_{(blkid + 2) * 4 + 3}"])
                            else:
                                t.wait_ge(sV.sem, M[f"vext{NT - 1}"])
                        last = None
                        for n_, kap in enumerate(kaps):
                            slot = kap % 6
                            last = t.matmul(
                                avp[0:65, (phi % 4) * 128:(phi % 4 + 1) * 128],
                                v_ext[:, 1040 * kap + h * 65:
                                      1040 * kap + h * 65 + 65],
                                probs[:, slot * 640 + (phi - kap) * 128:
                                      slot * 640 + (phi - kap + 1) * 128],
                                start=(phi % 4 == 0 and n_ == 0),
                                stop=(phi % 4 == 3 and n_ == len(kaps) - 1))
                        sP.inc(last)
                        M[f"av{h}_{phi}"] = sP.v

                    PE(pe_a)

                    if phi % 4 == 3:
                        bstart = phi - 3

                        def v_rcp(v, h=h, phi=phi, avp=avp):
                            v.wait_ge(sP.sem, M[f"av{h}_{phi}"])
                            pv = M.get(f"outer{h}_{phi - 4}") or (
                                M.get(f"outer{h - 1}_15") if h > 0 else None)
                            if pv:
                                v.wait_ge(sP.sem, pv)
                            sV.inc(v.reciprocal(rcp_s[0:1, 0:512],
                                                avp[64:65, 0:512]))
                            M[f"rcp{h}_{phi}"] = sV.v

                        V(v_rcp)

                        def pe_outer(t, h=h, phi=phi):
                            t.wait_ge(sV.sem, M[f"rcp{h}_{phi}"])
                            pv = M.get(f"rcpBe{h}_{phi - 4}") or (
                                M.get(f"rcpBe{h - 1}_15") if h > 0 else None)
                            if pv:
                                t.wait_ge(sV.sem, pv)
                            if h == 0 and phi == 3:
                                t.wait_ge(sV.sem, M[f"lgev{NT - 1}"])
                            if M.get("routing_psG_v") and (h, phi) == M["first_after_routing"]:
                                t.wait_ge(sV.sem, M["routing_psG_v"])
                            sP.inc(t.matmul(psG[0:64, 0:512], onesf[0:1, 0:64],
                                            rcp_s[0:1, 0:512], start=True,
                                            stop=True))
                            M[f"outer{h}_{phi}"] = sP.v

                        PE(pe_outer)

                        def v_nrm(v, h=h, phi=phi, avp=avp, bstart=bstart,
                                  p=p, sub=sub):
                            v.wait_ge(sP.sem, M[f"outer{h}_{phi}"])
                            sV.inc(v.tensor_copy(rcpB[:], psG[0:64, 0:512]))
                            M[f"rcpBe{h}_{phi}"] = sV.v
                            v.wait_ge(sV.sem, sV.v)
                            sV.inc(v.tensor_mul(
                                aoT[sub * 64:sub * 64 + 64,
                                    p * 2048 + bstart * 128:
                                    p * 2048 + (bstart + 4) * 128],
                                avp[0:64, 0:512], rcpB[:]))
                            M[f"nrm{h}_{phi}"] = sV.v

                        V(v_nrm)

            M[f"head_done{h}"] = sP.v

        def emit_routing():
            @GP
            def _(g):
                g.wait_ge(cc1, 1)
                sIdsL.inc(g.dma_start(
                    out=idsL[:],
                    in_=ids_all_d.rearrange("(p a) x -> p (a x)", p=128)), 16)
                M["idsL"] = sIdsL.v

            @PE
            def _(t):
                t.wait_ge(sIdsL.sem, M["idsL"])
                t.wait_ge(sV.sem, M["rcpBe1_15"])
                sP.inc(t.matmul(psG[:, 0:128], idsL[:], identf[:], is_transpose=True,
                         start=True, stop=True))
                M["idsT_mm"] = sP.v

            @V
            def _(v):
                v.wait_ge(sP.sem, M["idsT_mm"])
                v.tensor_copy(idsTa[:], psG[:, 0:128])
                v.memset(zrow[:], 0.0)
                sV.inc(v.memset(z8[:], 0.0))
                M["idsTa"] = sV.v

            for grp in range(8):
                def v_oh(v, grp=grp):
                    v.wait_ge(sV.sem, M["idsTa"])
                    if grp > 0:
                        v.wait_ge(sP.sem, M[f"cntmm{grp - 1}"])
                    last = None
                    for jj in range(16):
                        j = grp * 16 + jj
                        last = v.tensor_scalar(obA[:, jj * 32:(jj + 1) * 32],
                                               iota32[:], idsTa[:, j:j + 1],
                                               None, ALU.is_equal)
                    sV.inc(last)
                    M[f"ohg{grp}"] = sV.v

                V(v_oh)

                def pe_cnt(t, grp=grp):
                    t.wait_ge(sV.sem, M[f"ohg{grp}"])
                    if grp > 0:
                        t.wait_ge(sV.sem, M[f"cnte{grp - 1}"])
                    last = None
                    for jj in range(16):
                        last = t.matmul(psG[:, jj * 32:(jj + 1) * 32], onesb[:],
                                        obA[:, jj * 32:(jj + 1) * 32],
                                        start=(jj == 0), stop=(jj == 15))
                    sP.inc(last)
                    M[f"cntmm{grp}"] = sP.v

                PE(pe_cnt)

                def v_cnte(v, grp=grp):
                    v.wait_ge(sP.sem, M[f"cntmm{grp}"])
                    sV.inc(v.tensor_copy(cntflat[0:1, grp * 512:(grp + 1) * 512],
                                         psG[0:1, 0:512]))
                    M[f"cnte{grp}"] = sV.v

                V(v_cnte)

            @GP
            def _(g):
                g.wait_ge(sV.sem, M["cnte7"])
                sCnt2.inc(g.dma_start(
                    out=bass.AP(counts_d.tensor, 0, [[4096, 1], [1, 4096]]),
                    in_=cntflat[:]), 16)
                g.wait_ge(sCnt2.sem, sCnt2.v)
                sCnt2.inc(g.dma_start(out=cnt128[:], in_=counts_d), 16)
                M["cnt128"] = sCnt2.v
                g.iota(io8[:], [[2048, 8]], channel_multiplier=0,
                       allow_small_or_imprecise_dtypes=True)
                sL.inc(g.iota(iota5[:], [[128, 5]], channel_multiplier=1,
                       allow_small_or_imprecise_dtypes=True))
                M["iotas"] = sL.v

            @PE
            def _(t):
                t.wait_ge(sCnt2.sem, M["cnt128"])
                t.wait_ge(sV.sem, M["cnte7"])
                sP.inc(t.matmul(psG[0:32, 0:128], cnt128[:], identf[:], is_transpose=True,
                         start=True, stop=True))
                M["cntT_mm"] = sP.v

            @V
            def _(v):
                v.wait_ge(sP.sem, M["cntT_mm"])
                sV.inc(v.tensor_copy(cntT[:], psG[0:32, 0:128]))
                v.wait_ge(sV.sem, sV.v)
                sV.inc(v.tensor_tensor_scan(gincl[:], cntT[:], zrow[:], 0.0,
                                            ALU.add, ALU.add))
                v.wait_ge(sV.sem, sV.v)
                sV.inc(v.tensor_sub(gexcl[:], gincl[:], cntT[:]))
                v.wait_ge(sV.sem, sV.v)
                _gi = gincl[:].ap[0][0]
                _ge = gexcl[:].ap[0][0]
                sV.inc(v.tensor_tensor(rkcnt[:],
                                       bass.AP(gincl, 15, [[_gi, 32], [16, 8]]),
                                       bass.AP(gexcl, 0, [[_ge, 32], [16, 8]]),
                                       ALU.subtract))
                M["rkcnt"] = sV.v

            @PE
            def _(t):
                t.wait_ge(sV.sem, M["rkcnt"])
                sP.inc(t.matmul(psG[0:8, 0:32], rkcnt[:], identf[0:32, 0:32], is_transpose=True,
                         start=True, stop=True))
                M["rk8_mm"] = sP.v

            @V
            def _(v):
                v.wait_ge(sP.sem, M["rk8_mm"])
                sV.inc(v.tensor_copy(rk8[:], psG[0:8, 0:32]))
                v.wait_ge(sV.sem, sV.v)
                sV.inc(v.tensor_tensor_scan(ls8i[:], rk8[:], z8[:], 0.0,
                                            ALU.add, ALU.add))
                v.wait_ge(sV.sem, sV.v)
                sV.inc(v.tensor_sub(ls8[:], ls8i[:], rk8[:]))
                M["ls8"] = sV.v

            @PE
            def _(t):
                t.wait_ge(sV.sem, M["ls8"])
                sP.inc(t.matmul(psG[0:32, 0:8], ls8[:], identf[0:8, 0:8], is_transpose=True,
                         start=True, stop=True))
                M["ls32_mm"] = sP.v

            @V
            def _(v):
                v.wait_ge(sP.sem, M["ls32_mm"])
                v.wait_ge(sL.sem, M["iotas"])
                sV.inc(v.tensor_copy(ls32[:], psG[0:32, 0:8]))
                v.wait_ge(sV.sem, sV.v)
                _ge = gexcl[:].ap[0][0]
                gs_sl = bass.AP(gexcl, 0, [[_ge, 32], [16, 8]])
                sV.inc(v.tensor_copy(gs32[:], gs_sl))
                v.wait_ge(sV.sem, sV.v)
                sV.inc(v.tensor_sub(dl32[:], ls32[:], gs32[:]))
                v.wait_ge(sV.sem, sV.v)
                sV.inc(v.tensor_add(dl32[:], dl32[:], io8[:]))
                v.wait_ge(sV.sem, sV.v)
                _dl = dl32[:].ap[0][0]
                v.tensor_copy(dlD[:, 0:1], dl32[:, 0:1])
                sV.inc(v.tensor_tensor(dlD[:, 1:8],
                                       dl32[:, 1:8],
                                       bass.AP(dl32, 0, [[_dl, 32], [1, 7]]),
                                       ALU.subtract))
                v.wait_ge(sV.sem, sV.v)
                _ls = ls32[:].ap[0][0]
                gsRep = bass.AP(gexcl, 0, [[_ge, 32], [16, 8], [0, 16]])
                lsRep = bass.AP(ls32, 0, [[_ls, 32], [1, 8], [0, 16]])
                sV.inc(v.tensor_tensor(cmpT32[:], gexcl[:], gsRep, ALU.subtract))
                v.wait_ge(sV.sem, sV.v)
                sV.inc(v.tensor_tensor(cmpT32[:], cmpT32[:], lsRep, ALU.add))
                M["cmpT32"] = sV.v

            @PE
            def _(t):
                t.wait_ge(sV.sem, M["cmpT32"])
                sP.inc(t.matmul(psG[:, 0:32], gexcl[:], identf[0:32, 0:32], is_transpose=True,
                         start=True, stop=True))
                M["gcrow_mm"] = sP.v

            @V
            def _(v):
                v.wait_ge(sP.sem, M["gcrow_mm"])
                sV.inc(v.tensor_copy(gcrow[:], psG[:, 0:32]))
                M["gcrow"] = sV.v

            @PE
            def _(t):
                t.wait_ge(sV.sem, M["gcrow"])
                sP.inc(t.matmul(psG[:, 0:32], cmpT32[:], identf[0:32, 0:32], is_transpose=True,
                         start=True, stop=True))
                M["cmprow_mm"] = sP.v

            @V
            def _(v):
                v.wait_ge(sP.sem, M["cmprow_mm"])
                sV.inc(v.tensor_copy(cmprow[:], psG[:, 0:32]))
                M["cmprow"] = sV.v

            @GP
            def _(g):
                g.wait_ge(sV.sem, M["cmprow"])
                sTbl.inc(g.dma_start(out=gcar_d, in_=gcrow[:]), 16)
                sTbl.inc(g.dma_start(out=cmp_d, in_=cmprow[:]), 16)
                sTbl.inc(g.dma_start(out=gs8_d, in_=gs32[:]), 16)
                sTbl.inc(g.dma_start(out=dl8_d, in_=dlD[:]), 16)
                g.wait_ge(sTbl.sem, sTbl.v)
                with g.register("rc") as rc:
                    g.reg_load(rc, cid_in[0:1, 0:1])
                    with g.register("r1") as r1:
                        g.reg_mul(r1, rc, 16 * 32)
                        sBc.inc(g.dma_start(
                            out=gcarB[:],
                            in_=bass.AP(gcar_d.tensor, r1, [[0, 128], [1, 512]])), 16)
                        sBc.inc(g.dma_start(
                            out=cmpB[:],
                            in_=bass.AP(cmp_d.tensor, r1, [[0, 128], [1, 512]])), 16)
                    with g.register("r2") as r2:
                        g.reg_mul(r2, rc, 4 * 8)
                        sBc.inc(g.dma_start(
                            out=gsB[:],
                            in_=bass.AP(gs8_d.tensor, r2, [[0, 128], [1, 32]])), 16)
                        sBc.inc(g.dma_start(
                            out=dlB[:],
                            in_=bass.AP(dl8_d.tensor, r2, [[0, 128], [1, 32]])), 16)
                M["bcasts"] = sBc.v

            @V
            def _(v):
                v.wait_ge(sV.sem, M["cmprow"])
                last = None
                for j in range(NT):
                    last = v.tensor_scalar(obA[:, j * 32:(j + 1) * 32], iota32[:],
                                           idsT_own[:, j:j + 1], None,
                                           ALU.is_equal)
                sV.inc(last)
                M["ohown"] = sV.v

            @PE
            def _(t):
                t.wait_ge(sV.sem, M["ohown"])
                last = None
                for j in range(NT):
                    last = t.matmul(psG[:, j * 32:(j + 1) * 32], tri[:],
                                    obA[:, j * 32:(j + 1) * 32],
                                    start=(j == 0), stop=(j == NT - 1))
                sP.inc(last)
                M["pref_mm"] = sP.v

            @V
            def _(v):
                v.wait_ge(sP.sem, M["pref_mm"])
                v.wait_ge(sBc.sem, M["bcasts"])
                sV.inc(v.tensor_copy(prefA[:], psG[:, 0:512]))
                M["prefA"] = sV.v
                M["routing_psG_v"] = sV.v

            for j in range(NT):
                def v_slot(v, j=j):
                    v.wait_ge(sV.sem, M["prefA"])
                    if j > 0:
                        v.wait_ge(sV.sem, M[f"slot{j - 1}"])
                    v.tensor_scalar(oh1[:], iota32[:], idsT_own[:, j:j + 1], None,
                                    ALU.is_equal)
                    sV.inc(v.tensor_add(sc1[:], prefA[:, j * 32:(j + 1) * 32],
                                        gcarB[:, j * 32:(j + 1) * 32]))
                    v.wait_ge(sV.sem, sV.v)
                    sV.inc(v.tensor_mul(sc1[:], sc1[:], oh1[:]))
                    v.wait_ge(sV.sem, sV.v)
                    sV.inc(v.reduce_sum(sv1[:], sc1[:], AX.X))
                    v.wait_ge(sV.sem, sV.v)
                    sV.inc(v.tensor_scalar(sv2[:], idsT_own[:, j:j + 1], 640.0,
                                            -1.0, ALU.mult, ALU.add))
                    v.wait_ge(sV.sem, sV.v)
                    v.tensor_add(sv2[:], sv2[:], sv1[:])
                    sV.inc(v.tensor_scalar(svm[:], sv1[:], 640.0, None, ALU.is_le))
                    v.wait_ge(sV.sem, sV.v)
                    sV.inc(v.select(sv3[:], svm[:], sv2[:], c2e4[:], add_drain=True))
                    v.wait_ge(sV.sem, sV.v)
                    sV.inc(v.tensor_scalar(gslotI[:, j:j + 1], sv3[:], 0.0, None,
                                           ALU.add))
                    v.wait_ge(sV.sem, sV.v)
                    sV.inc(v.tensor_add(sc1[:], prefA[:, j * 32:(j + 1) * 32],
                                        cmpB[:, j * 32:(j + 1) * 32]))
                    v.wait_ge(sV.sem, sV.v)
                    sV.inc(v.tensor_mul(sc1[:], sc1[:], oh1[:]))
                    v.wait_ge(sV.sem, sV.v)
                    sV.inc(v.reduce_sum(sv1[:], sc1[:], AX.X))
                    v.wait_ge(sV.sem, sV.v)
                    sV.inc(v.tensor_scalar(cslotI[:, j:j + 1], sv1[:], -1.0, None,
                                           ALU.add))
                    M[f"slot{j}"] = sV.v

                V(v_slot)

                def gp_scat(g, j=j):
                    g.wait_ge(sV.sem, M[f"slot{j}"])
                    sSc.inc(g.dma_start(out=xsc[:],
                                        in_=xnbf_d[j * 128:(j + 1) * 128, :]), 16)
                    g.wait_ge(sSc.sem, sSc.v)
                    sSc.inc(g.indirect_dma_start(
                        xs_own_d[0:T, :],
                        IndirectOffsetOnAxis(cslotI[:, j:j + 1], 0),
                        xsc[:], None), 16)
                    g.wait_ge(sSc.sem, sSc.v)
                    M[f"scat{j}"] = sSc.v

                GP(gp_scat)

            @GP
            def _(g):
                g.wait_ge(sSc.sem, M[f"scat{NT - 1}"])
                g.collective_compute("AllGather", ALU.bypass, replica_groups=RG,
                                     ins=[xs_own_d], outs=[xs_all_d]).then_inc(cc2)

            @V
            def _(v):
                v.wait_ge(sBc.sem, M["bcasts"])
                v.wait_ge(sL.sem, M["iotas"])
                v.wait_ge(sV.sem, M[f"slot{NT - 1}"])
                for j in range(EL):
                    for s5 in range(5):
                        col = j * 5 + s5
                        v.wait_ge(sV.sem, sV.v)
                        sV.inc(v.tensor_scalar(sv1[:], iota5[:, s5:s5 + 1],
                                               dlB[:, j * 8:j * 8 + 1], None,
                                               ALU.add))
                        for cc_ in range(1, 8):
                            v.wait_ge(sV.sem, sV.v)
                            sV.inc(v.tensor_scalar(
                                sv2[:], iota5[:, s5:s5 + 1],
                                gsB[:, j * 8 + cc_:j * 8 + cc_ + 1],
                                None, ALU.is_ge))
                            v.wait_ge(sV.sem, sV.v)
                            sV.inc(v.scalar_tensor_tensor(
                                sv1[:], sv2[:], dlB[:, j * 8 + cc_:j * 8 + cc_ + 1],
                                sv1[:], ALU.mult, ALU.add))
                        v.wait_ge(sV.sem, sV.v)
                        sV.inc(v.tensor_scalar(sv1[:], sv1[:], 16383.0, None,
                                               ALU.min))
                        v.wait_ge(sV.sem, sV.v)
                        sV.inc(v.tensor_scalar(bufidxI[:, col:col + 1], sv1[:],
                                               0.0, None, ALU.add))
                M["bufidx"] = sV.v

        emit_pair_transposes(0)
        emit_head(0)
        emit_head(1)
        emit_pair_transposes(1)
        emit_routing()
        M["first_after_routing"] = (2, 3)
        for h in range(2, NHEADS):
            emit_head(h)
            if h % 2 == 1 and h + 1 < NHEADS:
                emit_pair_transposes((h + 1) // 2)

        M["attn_v_pre"] = sV.v
        if dbg:
            @GP
            def _(g):
                g.wait_ge(sV.sem, M[f"rope{NT - 1}"])
                g.wait_ge(sV.sem, M[f"vext{NT - 1}"])
                sDbg.inc(g.dma_start(out=DOUT["d_qkb"], in_=qkb[:, 0:4096]), 16)
                sDbg.inc(g.dma_start(out=DOUT["d_vext"], in_=v_ext[:, 0:2080]), 16)
                if TRUNC:
                    g.wait_ge(sV.sem, M["attn_v_pre"])
                    sDbg.inc(g.dma_start(out=DOUT["d_probs"], in_=probs[:]), 16)
                    sDbg.inc(g.dma_start(out=DOUT["d_tT"], in_=tT[:]), 16)
                    sDbg.inc(g.dma_start(out=DOUT["d_rcpB"], in_=rcpB[:]), 16)
                    g.wait_ge(sV.sem, M["attn_v_pre"])
                    sDbg.inc(g.dma_start(
                        out=DOUT["d_aoT"][0:128, 0:NHEADS * 1024],
                        in_=aoT[:, 0:NHEADS * 1024]), 16)
                M["dbg_early"] = sDbg.v

        M["attn_v"], M["attn_s"], M["attn_p"] = sV.v, sS.v, sP.v
        pa.close()
        pat.close()

        if TRUNC:
            @SY
            def _(s):
                s.wait_ge(sDbg.sem, M["dbg_early"])
            return nc, DOUT
        pm = ExitStack()
        SBM = lambda name, shape, dt: pm.enter_context(nc.sbuf_tensor(name, shape, dt))
        wgu_bf = SBM("wgu_bf", [128, 2 * NC * 1024], BF16)
        wd_bf = SBM("wd_bf", [128, NC * 1024], BF16)
        buf_s = SBM("buf_s", [128, 5 * 1024], BF16)
        bufT_s = SBM("bufT_s", [128, NC * 640], BF16)
        mid_s = SBM("mid_s", [128, NC * 640], BF16)
        silu_s = SBM("silu_s", [128, 640], F32)
        ybf = SBM("ybf", [128, 1024], BF16)

        wg_r = wg_in.rearrange("e (a p) f -> e p a f", p=128)
        wu_r = wu_in.rearrange("e (a p) f -> e p a f", p=128)
        wd_r = wd_in.rearrange("e (a p) f -> e p a f", p=128)

        for j in range(EL):
            def gp_wld(g, j=j):
                g.wait_ge(sV.sem, M["attn_v"])
                g.wait_ge(sS.sem, M["attn_s"])
                if dbg and j == 0:
                    g.wait_ge(sDbg.sem, M["dbg_early"])
                if j > 0:
                    g.wait_ge(sP.sem, M[f"gu_mm{j - 1}"])
                sWgu.inc(g.dma_start(out=wgu_bf[:, 0:8192], in_=wg_r[j]), 16)
                sWgu.inc(g.dma_start(out=wgu_bf[:, 8192:16384], in_=wu_r[j]), 16)
                M[f"wgu{j}"] = sWgu.v
                if j > 0:
                    g.wait_ge(sP.sem, M[f"y_mm{j - 1}_4"])
                sWd.inc(g.dma_start(out=wd_bf[:], in_=wd_r[j]), 16)
                M[f"wd{j}"] = sWd.v
                g.wait_ge(cc2, 1)
                g.wait_ge(sV.sem, M["bufidx"])
                if j > 0:
                    g.wait_ge(sP.sem, M[f"bufT_mm{j - 1}"])
                for s5 in range(5):
                    sBuf.inc(g.indirect_dma_start(
                        buf_s[:, s5 * 1024:(s5 + 1) * 1024], None,
                        xs_all_d[0:8 * T, :],
                        IndirectOffsetOnAxis(bufidxI[:, j * 5 + s5:j * 5 + s5 + 1],
                                             0)), 16)
                M[f"buf{j}"] = sBuf.v

            GP(gp_wld)

            for k in range(NC):
                srcs = [buf_s[:, s5 * 1024 + k * 128: s5 * 1024 + (k + 1) * 128]
                        for s5 in range(5)]
                wpe = [(sBuf.sem, M[f"buf{j}"])]
                wl = []
                if j > 0:
                    wl.append((sP.sem, M[f"gu_mm{j - 1}"]))
                mpe, mev = transpose_batch(srcs, bufT_s[:, k * 640:(k + 1) * 640],
                                           wait_v=wl, wait_pe=wpe)
            M[f"bufT_mm{j}"] = sP.v
            M[f"bufT{j}"] = sV.v

            for f8 in range(NC):
                def pe_gu(t, j=j, f8=f8):
                    t.wait_ge(sV.sem, M[f"bufT{j}"])
                    t.wait_ge(sWgu.sem, M[f"wgu{j}"])
                    if j == 0 and f8 == 0:
                        t.wait_ge(sS.sem, M["attn_s"])
                    pv = M.get(f"mid{j}_{f8 - 1}") or (
                        M.get(f"mid{j - 1}_{NC - 1}") if j > 0 else None)
                    if pv:
                        t.wait_ge(sV.sem, pv)
                    for k in range(NC):
                        lhs = wgu_bf[:, k * 1024 + f8 * 128: k * 1024 + (f8 + 1) * 128]
                        rhsT = bufT_s[:, k * 640:(k + 1) * 640]
                        t.matmul(scoreP[0][:, 0:512], lhs, rhsT[:, 0:512],
                                 start=(k == 0), stop=(k == NC - 1))
                        t.matmul(scoreP[0][:, 512:640], lhs, rhsT[:, 512:640],
                                 start=(k == 0), stop=(k == NC - 1))
                        lhsu = wgu_bf[:, 8192 + k * 1024 + f8 * 128:
                                      8192 + k * 1024 + (f8 + 1) * 128]
                        t.matmul(scoreP[1][:, 0:512], lhsu, rhsT[:, 0:512],
                                 start=(k == 0), stop=(k == NC - 1))
                        last = t.matmul(scoreP[1][:, 512:640], lhsu,
                                        rhsT[:, 512:640],
                                        start=(k == 0), stop=(k == NC - 1))
                    sP.inc(last)
                    M[f"gu{j}_{f8}"] = sP.v
                    if f8 == NC - 1:
                        M[f"gu_mm{j}"] = sP.v

                PE(pe_gu)

                def sc_silu(s, j=j, f8=f8):
                    s.wait_ge(sP.sem, M[f"gu{j}_{f8}"])
                    sS.inc(s.activation(silu_s[:], scoreP[0][0:128, 0:640],
                                        AF.Silu, bias=zt[:], scale=1.0))
                    M[f"silu{j}_{f8}"] = sS.v

                SC(sc_silu)

                def v_mid(v, j=j, f8=f8):
                    v.wait_ge(sS.sem, M[f"silu{j}_{f8}"])
                    if j > 0 and f8 == 0:
                        v.wait_ge(sP.sem, M[f"y_mm{j - 1}_4"])
                    sV.inc(v.tensor_mul(mid_s[:, f8 * 640:(f8 + 1) * 640],
                                        silu_s[:], scoreP[1][0:128, 0:640]))
                    M[f"mid{j}_{f8}"] = sV.v

                V(v_mid)

            for s5 in range(5):
                def pe_y(t, j=j, s5=s5):
                    t.wait_ge(sV.sem, M[f"mid{j}_{NC - 1}"])
                    if j == 0 and s5 == 0:
                        t.wait_ge(sV.sem, M["attn_v"])
                    t.wait_ge(sWd.sem, M[f"wd{j}"])
                    pv = M.get(f"yev{j}_{s5 - 1}") or (
                        M.get(f"yev{j - 1}_4") if j > 0 else None)
                    if pv is None and j == 0:
                        pv = M["attn_s"]
                    t.wait_ge(sS.sem, pv)
                    for k in range(NC):
                        lhs = mid_s[:, k * 640 + s5 * 128: k * 640 + (s5 + 1) * 128]
                        t.matmul(psE[:, 0:512], lhs,
                                 wd_bf[:, k * 1024:k * 1024 + 512],
                                 start=(k == 0), stop=(k == NC - 1))
                        last = t.matmul(psF[:, 0:512], lhs,
                                        wd_bf[:, k * 1024 + 512:(k + 1) * 1024],
                                        start=(k == 0), stop=(k == NC - 1))
                    sP.inc(last)
                    M[f"y_mm{j}_{s5}"] = sP.v

                PE(pe_y)

                def sc_yev(s, j=j, s5=s5):
                    s.wait_ge(sP.sem, M[f"y_mm{j}_{s5}"])
                    pv = M.get(f"yst{j}_{s5 - 1}") or (
                        M.get(f"yst{j - 1}_4") if j > 0 else None)
                    if pv:
                        s.wait_ge(sY.sem, pv)
                    s.copy(ybf[:, 0:512], psE[:, 0:512])
                    sS.inc(s.copy(ybf[:, 512:1024], psF[:, 0:512]))
                    M[f"yev{j}_{s5}"] = sS.v
                    s.wait_ge(sS.sem, sS.v)
                    row = j * CAP + s5 * 128
                    sY.inc(s.dma_start(out=y_own_d[row:row + 128, :], in_=ybf[:]),
                           16)
                    M[f"yst{j}_{s5}"] = sY.v

                SC(sc_yev)

        @GP
        def _(g):
            g.wait_ge(sY.sem, M[f"yst{EL - 1}_4"])
            g.collective_compute("AllGather", ALU.bypass, replica_groups=RG,
                                 ins=[y_own_d], outs=[y_all_d]).then_inc(cc3)

        M["moe_v"], M["moe_s"], M["moe_p"] = sV.v, sS.v, sP.v
        pm.close()

        pf = ExitStack()
        SBF = lambda name, shape, dt: pf.enter_context(nc.sbuf_tensor(name, shape, dt))
        wo_bf = SBF("wo_bf", [128, NC * 1024], BF16)
        xf_t = SBF("xf_t", [128, 2 * 1024], F32)
        xnf_t = SBF("xnf_t", [128, 2 * 1024], F32)
        moe_t = SBF("moe_t", [128, 2 * 1024], BF16)
        asm1 = SBF("asm1", [128, 1024], F32)
        outt = SBF("outt", [128, 2 * 1024], F32)

        wout_r = wout_in.rearrange("(a p) f -> p a f", p=128)

        @GP
        def _(g):
            g.wait_ge(sV.sem, M["moe_v"])
            g.wait_ge(sS.sem, M["moe_s"])
            sWo.inc(g.dma_start(out=wo_bf[:], in_=wout_r), 16)
            M["wobf"] = sWo.v

        for i in range(NT):
            sl = i % 2

            def sy_ld(s, i=i, sl=sl):
                s.wait_ge(sWo.sem, M["wobf"])
                if i >= 2:
                    s.wait_ge(sV.sem, M[f"asm{i - 2}"])
                sF[sl].inc(s.dma_start(out=xf_t[:, sl * 1024:(sl + 1) * 1024],
                                       in_=x_in[i * 128:(i + 1) * 128, :]), 16)
                sF[sl].inc(s.dma_start(out=xnf_t[:, sl * 1024:(sl + 1) * 1024],
                                       in_=xn_d[i * 128:(i + 1) * 128, :]), 16)
                M[f"fld{i}"] = sF[sl].v

            SY(sy_ld)

            def gp_gb(g, i=i, sl=sl):
                g.wait_ge(cc3, 1)
                if i >= 2:
                    g.wait_ge(sV.sem, M[f"asm{i - 2}"])
                sL.inc(g.memset(moe_t[:, sl * 1024:(sl + 1) * 1024], 0.0))
                g.wait_ge(sL.sem, sL.v)
                sGb[sl].inc(g.indirect_dma_start(
                    moe_t[:, sl * 1024:(sl + 1) * 1024], None,
                    y_all_d[0:NE_T, :],
                    IndirectOffsetOnAxis(gslotI[:, i:i + 1], 0),
                    bounds_check=NE_T - 1, oob_is_err=False), 16)
                M[f"gb{i}"] = sGb[sl].v

            GP(gp_gb)

            def pe_wo(t, i=i):
                t.wait_ge(sWo.sem, M["wobf"])
                if i == 0:
                    t.wait_ge(sS.sem, M["moe_s"])
                if i > 0:
                    t.wait_ge(sV.sem, M[f"asmv{i - 1}"])
                last = None
                for k in range(NC):
                    lhs = aoT[:, k * 2048 + i * 128: k * 2048 + (i + 1) * 128]
                    t.matmul(psE[:, 0:512], lhs, wo_bf[:, k * 1024:k * 1024 + 512],
                             start=(k == 0), stop=(k == NC - 1))
                    last = t.matmul(psF[:, 0:512], lhs,
                                    wo_bf[:, k * 1024 + 512:(k + 1) * 1024],
                                    start=(k == 0), stop=(k == NC - 1))
                sP.inc(last)
                M[f"wo_mm{i}"] = sP.v

            PE(pe_wo)

            def v_asm(v, i=i, sl=sl):
                v.wait_ge(sP.sem, M[f"wo_mm{i}"])
                v.wait_ge(sF[sl].sem, M[f"fld{i}"])
                v.wait_ge(sGb[sl].sem, M[f"gb{i}"])
                v.scalar_tensor_tensor(asm1[:, 0:512],
                                       moe_t[:, sl * 1024:sl * 1024 + 512],
                                       gates[:, i:i + 1], psE[:, 0:512],
                                       ALU.mult, ALU.add)
                sV.inc(v.scalar_tensor_tensor(
                    asm1[:, 512:1024],
                    moe_t[:, sl * 1024 + 512:(sl + 1) * 1024],
                    gates[:, i:i + 1], psF[:, 0:512],
                    ALU.mult, ALU.add))
                v.wait_ge(sV.sem, sV.v)
                sV.inc(v.tensor_add(outt[:, sl * 1024:(sl + 1) * 1024],
                                    xf_t[:, sl * 1024:(sl + 1) * 1024],
                                    xnf_t[:, sl * 1024:(sl + 1) * 1024]))
                v.wait_ge(sV.sem, sV.v)
                sV.inc(v.tensor_add(outt[:, sl * 1024:(sl + 1) * 1024],
                                    outt[:, sl * 1024:(sl + 1) * 1024], asm1[:]))
                M[f"asmv{i}"] = sV.v

            V(v_asm)

            def sc_ost(s, i=i, sl=sl):
                s.wait_ge(sV.sem, M[f"asmv{i}"])
                sOs[sl].inc(s.dma_start(out=out_d[i * 128:(i + 1) * 128, :],
                                        in_=outt[:, sl * 1024:(sl + 1) * 1024]), 16)
                M[f"ost{i}"] = sOs[sl].v

            SC(sc_ost)

            def v_asm_done(v, i=i, sl=sl):
                v.wait_ge(sOs[sl].sem, M[f"ost{i}"])
                sV.inc(v.nop())
                M[f"asm{i}"] = sV.v

            V(v_asm_done)

        if dbg:
            @GP
            def _(g):
                g.wait_ge(sOs[0].sem, M[f"ost{NT - 2}"])
                g.wait_ge(sOs[1].sem, M[f"ost{NT - 1}"])
                sDbg.inc(g.dma_start(out=DOUT["d_xn"], in_=xn_d), 16)
                sDbg.inc(g.dma_start(out=DOUT["d_ids"], in_=ids_own_d), 16)
                sDbg.inc(g.dma_start(out=DOUT["d_gates"], in_=gates[:]), 16)
                sDbg.inc(g.dma_start(out=DOUT["d_gslot"], in_=gslotI[:]), 16)
                sDbg.inc(g.dma_start(out=DOUT["d_bufidx"], in_=bufidxI[:]), 16)
                sDbg.inc(g.dma_start(out=DOUT["d_counts"], in_=counts_d), 16)
                sDbg.inc(g.dma_start(out=DOUT["d_gcar"], in_=gcar_d), 16)
                sDbg.inc(g.dma_start(out=DOUT["d_cmp"], in_=cmp_d), 16)
                sDbg.inc(g.dma_start(out=DOUT["d_gs8"], in_=gs8_d), 16)
                sDbg.inc(g.dma_start(out=DOUT["d_dl8"], in_=dl8_d), 16)
                sDbg.inc(g.dma_start(out=DOUT["d_xs"], in_=xs_own_d), 16)
                sDbg.inc(g.dma_start(out=DOUT["d_y"], in_=y_own_d), 16)
                M["dbg"] = sDbg.v

        @SY
        def _(s):
            s.wait_ge(sOs[0].sem, M[f"ost{NT - 2}"])
            s.wait_ge(sOs[1].sem, M[f"ost{NT - 1}"])
            if dbg:
                s.wait_ge(sDbg.sem, M["dbg"])

        pf.close()
        pao.close()

    return nc, DOUT


_NC_CACHE = {}


def kernel(L0_attn_x, L0_attn_Wqkv, L0_attn_Wout, L0_ffn_w_router,
           L0_ffn_w_gate, L0_ffn_w_up, L0_ffn_w_down, dbg=False):
    key = bool(dbg)
    if key not in _NC_CACHE:
        _NC_CACHE[key] = build(dbg=dbg)
    nc, DOUT = _NC_CACHE[key]
    x = np.ascontiguousarray(np.asarray(L0_attn_x, np.float32))
    wqkv = np.ascontiguousarray(np.asarray(L0_attn_Wqkv, np.float32))
    wout = np.ascontiguousarray(np.asarray(L0_attn_Wout, np.float32))
    wr = np.ascontiguousarray(np.asarray(L0_ffn_w_router, np.float32))
    wg = np.ascontiguousarray(np.asarray(L0_ffn_w_gate, np.float32))
    wu = np.ascontiguousarray(np.asarray(L0_ffn_w_up, np.float32))
    wd = np.ascontiguousarray(np.asarray(L0_ffn_w_down, np.float32))
    in_maps = []
    for c in range(8):
        in_maps.append({
            "x": x[c], "wqkv": wqkv, "wout": wout, "wr": wr,
            "wg": wg[4 * c:4 * c + 4], "wu": wu[4 * c:4 * c + 4],
            "wd": wd[4 * c:4 * c + 4],
            "cid": np.array([[c]], np.int32),
        })
    res = run_bass_kernel_spmd(nc, in_maps, core_ids=list(range(8)))
    out = np.stack([res.results[c]["out"] for c in range(8)], axis=0)
    if dbg:
        return out, res.results
    return out
